# version 04 (final)
# baseline (speedup 1.0000x reference)
import os
import sys
for _p in ("/opt/trn_rl_repo",):
    if _p not in sys.path:
        sys.path.insert(0, _p)
import numpy as np
import concourse.bass as bass
import concourse.mybir as mybir
from concourse.bass_utils import run_bass_kernel_spmd
from contextlib import ExitStack

F32 = mybir.dt.float32
BF16 = mybir.dt.bfloat16
AF = mybir.ActivationFunctionType
ALU = mybir.AluOpType
AX = mybir.AxisListType.X

NCORES = 8
B, SEQ, D = 32, 2048, 1024
BPC = B // NCORES
TCORE = BPC * SEQ
NT = TCORE // 128
KC = D // 128
HEADS, HD = 16, 64
NPAIR = HEADS // 2
E_EXP, H_FF = 8, 2048
HC_FF = H_FF // 128
EPS = 1e-5
STAGE = int(os.environ.get("K_STAGE", "4"))

ENGS = ("sync", "gpsimd", "tensor", "vector", "scalar")


class K:
    def __init__(self, nc, ctx):
        self.nc = nc
        self.c = {}
        self.plan = {e: [] for e in ENGS}
        self.sems = {}
        for name in ("xd", "wd", "st", "T", "V", "S", "G"):
            self.sems[name] = ctx.enter_context(nc.semaphore(name))

    def bump(self, s, by=1):
        self.c[s] = self.c.get(s, 0) + by
        return self.c[s]

    def val(self, s):
        return self.c.get(s, 0)

    def on(self, eng, fn):
        self.plan[eng].append(fn)

    def wait(self, eng, s, v):
        if v <= 0:
            return
        sem = self.sems[s]
        self.on(eng, lambda E, sem=sem, v=v: E.wait_ge(sem, v))

    def dma(self, q, out, in_, s="wd"):
        v = self.bump(s, 16)
        sem = self.sems[s]
        self.on(q, lambda E, out=out, in_=in_, sem=sem:
                E.dma_start(out=out, in_=in_).then_inc(sem, 16))
        return v

    def op(self, eng, fn, s=None):
        if s is None:
            self.on(eng, lambda E, fn=fn: fn(E))
            return None
        v = self.bump(s, 1)
        sem = self.sems[s]
        self.on(eng, lambda E, fn=fn, sem=sem: fn(E).then_inc(sem, 1))
        return v

    def flush(self, block):
        for e in ENGS:
            lst = self.plan[e]
            if not lst:
                continue
            dec = getattr(block, e)

            def runner(E, lst=lst):
                for fn in lst:
                    fn(E)
            dec(runner)
            self.plan[e] = []

    def barrier(self):
        snap = [(s, self.val(s)) for s in self.sems]
        for e in ENGS:
            for s, v in snap:
                self.wait(e, s, v)


def build():
    nc = bass.Bass("TRN2", target_bir_lowering=False, detect_race_conditions=False)

    def din(name, shape):
        return nc.dram_tensor(name, list(shape), F32, kind="ExternalInput").ap()

    X0 = din("x0", (TCORE, D))
    Wq_d = din("Wq", (D, D)); bq_d = din("bq", (D,))
    Wk_d = din("Wk", (D, D)); bk_d = din("bk", (D,))
    Wv_d = din("Wv", (D, D)); bv_d = din("bv", (D,))
    Wp_d = din("Wp", (D, D)); bp_d = din("bp", (D,))
    GW_d = din("gw", (D, E_EXP))
    W1_d = din("w1", (E_EXP, D, H_FF)); B1_d = din("b1", (E_EXP, H_FF))
    LG1_d = din("lg1", (E_EXP, H_FF)); LB1_d = din("lb1", (E_EXP, H_FF))
    W2_d = din("w2", (E_EXP, H_FF, D)); B2_d = din("b2", (E_EXP, D))
    LG_d = din("lng", (D,)); LB_d = din("lnb", (D,))
    WK_d = din("WK", (D, D)); WQ_d = din("WQ", (D, D)); WV_d = din("WV", (D, D))
    WO_d = din("WO", (D, D))
    WF_d = din("Wf", (D, D)); A1_d = din("A1", (D, D)); A2_d = din("A2", (D, D))
    BF_d = din("bf", (D,))

    Xs = [X0,
          nc.dram_tensor("x1", [TCORE, D], F32, kind="Internal").ap(),
          nc.dram_tensor("x2", [TCORE, D], F32, kind="Internal").ap(),
          nc.dram_tensor("x3", [TCORE, D], F32, kind="Internal").ap()]
    Y = nc.dram_tensor("y", [TCORE, D], F32, kind="ExternalOutput").ap()

    def xdst(i):
        return Y if i == STAGE else Xs[i]

    ctx = ExitStack()
    with ctx:
        EN = ctx.enter_context
        k = K(nc, ctx)
        blk = EN(nc.Block())

        identF = EN(nc.sbuf_tensor("identF", [128, 128], F32))
        identB = EN(nc.sbuf_tensor("identB", [128, 128], BF16))
        ones_bf = EN(nc.sbuf_tensor("ones_bf", [1, 512], BF16))
        ones_col = EN(nc.sbuf_tensor("ones_col", [128, 1], BF16))
        ones_f = EN(nc.sbuf_tensor("ones_f", [1, 128], F32))

        vg = 0
        for t, v in ((identF, 1.0), (identB, 1.0), (ones_bf, 1.0),
                     (ones_col, 1.0), (ones_f, 1.0)):
            vg = k.op("gpsimd", lambda E, t=t, v=v: E.memset(t[:], v), s="G")
        k.wait("gpsimd", "G", vg)
        k.op("gpsimd", lambda E: E.affine_select(
            identF[:], identF[:], [[-1, 128]], ALU.is_equal, 0.0,
            base=0, channel_multiplier=1), s="G")
        vSETUP = k.op("gpsimd", lambda E: E.affine_select(
            identB[:], identB[:], [[-1, 128]], ALU.is_equal, 0.0,
            base=0, channel_multiplier=1), s="G")
        k.barrier()

        def w_lhsT_load(dram2d, sb, nsplit=2):
            kc = dram2d.shape[0] // 128
            per = kc // nsplit
            v = 0
            for i in range(nsplit):
                src = dram2d[i * per * 128:(i + 1) * per * 128, :].rearrange(
                    "(a p) n -> p a n", p=128)
                v = k.dma("gpsimd", sb[:, i * per:(i + 1) * per, :], src, "wd")
            return v

        def grp_view(X, g):
            return X[g * 512:(g + 1) * 512, :].rearrange("(t p) d -> p t d", p=128)

        def ln_stats(xg, vx, negmu, rr, c0, st8, sqscr, veps4):
            for t in range(4):
                k.wait("vector", "xd", vx)
                k.op("vector", lambda E, t=t: E.reduce_sum(
                    st8[:, t:t + 1], xg[:, t, :], AX))
                k.wait("scalar", "xd", vx)
                vs = k.op("scalar", lambda E, t=t: E.activation(
                    sqscr[:], xg[:, t, :], AF.Square,
                    accum_out=st8[:, 4 + t:5 + t]), s="S")
            k.wait("vector", "S", vs)
            vv = k.op("vector", lambda E: E.tensor_scalar(
                negmu[:, c0:c0 + 4], st8[:, 0:4], -1.0 / D, None, ALU.mult),
                s="V")
            k.wait("vector", "V", vv)
            vv = k.op("vector", lambda E: E.tensor_tensor(
                veps4[:], negmu[:, c0:c0 + 4], negmu[:, c0:c0 + 4], ALU.mult),
                s="V")
            k.wait("vector", "V", vv)
            vv = k.op("vector", lambda E: E.scalar_tensor_tensor(
                rr[:, c0:c0 + 4], st8[:, 4:8], 1.0 / D, veps4[:],
                ALU.mult, ALU.subtract), s="V")
            k.wait("vector", "V", vv)
            vv = k.op("vector", lambda E: E.tensor_scalar(
                veps4[:], rr[:, c0:c0 + 4], EPS, None, ALU.add), s="V")
            k.wait("scalar", "V", vv)
            vs = k.op("scalar", lambda E: E.activation(
                veps4[:], veps4[:], AF.Sqrt), s="S")
            k.wait("vector", "S", vs)
            vv = k.op("vector", lambda E: E.reciprocal(
                rr[:, c0:c0 + 4], veps4[:]), s="V")
            k.wait("vector", "V", vv)
            return vv

        def ln_apply(xg, y_bf, negmu, rr, c0, vx, vTprev):
            k.wait("vector", "xd", vx)
            k.wait("vector", "T", vTprev)
            vv = 0
            for t in range(4):
                vv = k.op("vector", lambda E, t=t: E.tensor_scalar(
                    y_bf[:, t, :], xg[:, t, :], negmu[:, c0 + t:c0 + t + 1],
                    rr[:, c0 + t:c0 + t + 1], ALU.add, ALU.mult), s="V")
            return vv

        def transpose_grp(y_bf, yT, psT, vy):
            k.wait("tensor", "V", vy)
            vv = k.val("V")
            for t in range(4):
                for fh in range(2):
                    k.wait("tensor", "V", vv)
                    vt = 0
                    for f4 in range(4):
                        f = fh * 4 + f4
                        vt = k.op("tensor", lambda E, t=t, f=f, f4=f4: E.transpose(
                            psT[:, f4, :], y_bf[:, t, f * 128:(f + 1) * 128],
                            identB[:]), s="T" if f4 == 3 else None)
                    k.wait("vector", "T", vt)
                    vv = k.op("vector", lambda E, t=t, fh=fh: E.tensor_copy(
                        yT[:, fh * 4:(fh + 1) * 4, t * 128:(t + 1) * 128],
                        psT[:, :, :]), s="V")
            return vv, vt

        with ExitStack() as ph:
            P = ph.enter_context
            wq_bf = P(nc.sbuf_tensor("wq_bf", [128, KC, D], BF16))
            wk_bf = P(nc.sbuf_tensor("wk_bf", [128, KC, D], BF16))
            wv_bf = P(nc.sbuf_tensor("wv_bf", [128, KC, D], BF16))
            wp_bf = P(nc.sbuf_tensor("wp_bf", [128, KC, D], BF16))
            bq_r = P(nc.sbuf_tensor("bq_r", [1, D], BF16))
            bk_r = P(nc.sbuf_tensor("bk_r", [1, D], BF16))
            bv_r = P(nc.sbuf_tensor("bv_r", [1, D], BF16))
            bp_r = P(nc.sbuf_tensor("bp_r", [1, D], BF16))
            xg = P(nc.sbuf_tensor("xg", [128, 2, 4, D], F32))
            y_bf = P(nc.sbuf_tensor("y_bf", [128, 4, D], BF16))
            yTg = P(nc.sbuf_tensor("yTg", [128, KC, 512], BF16))
            k_sb = P(nc.sbuf_tensor("k_sb", [128, 8, D], BF16))
            v_sb = P(nc.sbuf_tensor("v_sb", [128, 8, D], BF16))
            kv_acc = P(nc.sbuf_tensor("kv_acc", [128, NPAIR, HD], F32))
            kv_bf = P(nc.sbuf_tensor("kv_bf", [128, NPAIR, 2 * HD], BF16))
            q_bf = P(nc.sbuf_tensor("q_bf", [128, NPAIR, 512], BF16))
            att_bf = P(nc.sbuf_tensor("att_bf", [128, D], BF16))
            attT_bf = P(nc.sbuf_tensor("attT_bf", [128, KC, 128], BF16))
            x1sb = P(nc.sbuf_tensor("x1sb", [128, 2, 4, D], F32))
            scr1 = P(nc.sbuf_tensor("scr1", [128, 512], F32))
            scr2 = P(nc.sbuf_tensor("scr2", [128, 512], F32))
            negmu = P(nc.sbuf_tensor("negmu", [128, 16], F32))
            rr = P(nc.sbuf_tensor("rr", [128, 16], F32))
            st8 = P(nc.sbuf_tensor("st8", [128, 8], F32))
            veps4 = P(nc.sbuf_tensor("veps4", [128, 4], F32))
            sqscr = att_bf
            psA = P(nc.psum_tensor("psA", [128, 2, 512], F32))
            psB = P(nc.psum_tensor("psB", [128, 2, 512], F32))
            psT = P(nc.psum_tensor("psT", [128, 4, 128], BF16))

            k.op("gpsimd", lambda E: E.memset(kv_bf[:], 0.0), s="G")
            k.wait("tensor", "G", k.val("G"))
            vW = 0
            for w_sb, w_dr in ((wk_bf, Wk_d), (wv_bf, Wv_d), (wq_bf, Wq_d),
                               (wp_bf, Wp_d)):
                vW = w_lhsT_load(w_dr, w_sb)
            for r_sb, r_dr in ((bq_r, bq_d), (bk_r, bk_d), (bv_r, bv_d),
                               (bp_r, bp_d)):
                vW = k.dma("gpsimd", r_sb[0:1, :], r_dr[None, :], "wd")

            xg_free = [0, 0]
            x1_free = [0, 0]
            yT_T = [0]

            def load_xgrp(X, g, slot):
                k.wait("sync", "V", xg_free[slot])
                return k.dma("sync", xg[:, slot, :, :], grp_view(X, g), "xd")

            for sq in range(BPC):
                par_proj = [0]

                def proj_tile(yT_src, w_sb, bias_row, act, dst, dsti, vyT,
                              vprev_epi):
                    vals = []
                    for hcol in range(2):
                        sl = par_proj[0]; par_proj[0] ^= 1
                        k.wait("tensor", "V", vprev_epi[sl])
                        k.wait("tensor", "wd", vW)
                        k.wait("tensor", "V", vyT)
                        for kc in range(KC):
                            k.op("tensor", lambda E, kc=kc, sl=sl, hcol=hcol: E.matmul(
                                psA[:, sl, :], yT_src[:, kc, :],
                                w_sb[:, kc, hcol * 512:(hcol + 1) * 512],
                                start=(kc == 0), stop=False))
                        vt = k.op("tensor", lambda E, sl=sl, hcol=hcol: E.matmul(
                            psA[:, sl, :], ones_bf[0:1, 0:128],
                            bias_row[0:1, hcol * 512:(hcol + 1) * 512],
                            start=False, stop=True), s="T")
                        k.wait("vector", "T", vt)
                        if act == "elup1":
                            k.op("vector", lambda E, sl=sl: E.tensor_scalar(
                                scr1[:], psA[:, sl, :], 0.0, None, ALU.max))
                            vv = k.op("vector", lambda E, sl=sl: E.tensor_scalar(
                                scr2[:], psA[:, sl, :], 0.0, None, ALU.min), s="V")
                            k.wait("scalar", "V", vv)
                            vs = k.op("scalar", lambda E: E.activation(
                                scr2[:], scr2[:], AF.Exp), s="S")
                            k.wait("vector", "S", vs)
                            vv = k.op("vector", lambda E, hcol=hcol, dsti=dsti: E.tensor_tensor(
                                dst[:, dsti, hcol * 512:(hcol + 1) * 512],
                                scr1[:], scr2[:], ALU.add), s="V")
                        else:
                            vv = k.op("vector", lambda E, sl=sl, hcol=hcol, dsti=dsti:
                                      E.tensor_copy(
                                dst[:, dsti, hcol * 512:(hcol + 1) * 512],
                                psA[:, sl, :]), s="V")
                        vprev_epi[sl] = vv
                        vals.append(vv)
                    return vals[-1]

                epi = [0, 0]
                vkv_done = 0
                for half in range(2):
                    for gi in range(2):
                        g = sq * 4 + half * 2 + gi
                        slot = g % 2
                        vx = load_xgrp(X0, g, slot)
                        vst = ln_stats(xg[:, slot], vx, negmu, rr,
                                       (half * 2 + gi) * 4, st8, sqscr, veps4)
                        vy = ln_apply(xg[:, slot], y_bf, negmu, rr,
                                      (half * 2 + gi) * 4, vx, yT_T[0])
                        xg_free[slot] = vy
                        vyT, yT_T[0] = transpose_grp(y_bf, yTg, psT, vy)
                        for t in range(4):
                            ht = gi * 4 + t
                            yT_t = yTg[:, :, t * 128:(t + 1) * 128]
                            proj_tile(yT_t, wk_bf, bk_r, "elup1", k_sb, ht,
                                      vyT, epi)
                            proj_tile(yT_t, wv_bf, bv_r, "copy", v_sb, ht,
                                      vyT, epi)
                    vkv_epi = [k.val("V"), k.val("V")]
                    vlast_kv = k.val("V")
                    for h in range(HEADS):
                        sl = h % 2
                        k.wait("tensor", "V", vkv_epi[sl])
                        if h < 2:
                            k.wait("tensor", "V", vlast_kv)
                        for ht in range(8):
                            vt = k.op("tensor", lambda E, h=h, ht=ht, sl=sl: E.matmul(
                                psB[0:64, sl, 0:64],
                                k_sb[:, ht, h * 64:(h + 1) * 64],
                                v_sb[:, ht, h * 64:(h + 1) * 64],
                                start=(ht == 0), stop=(ht == 7)),
                                s="T" if ht == 7 else None)
                        k.wait("vector", "T", vt)
                        p, hp = h // 2, (h % 2) * 64
                        if half == 0:
                            vv = k.op("vector", lambda E, p=p, hp=hp, sl=sl:
                                      E.tensor_copy(kv_acc[hp:hp + 64, p, :],
                                                    psB[0:64, sl, 0:64]), s="V")
                        else:
                            vv = k.op("vector", lambda E, p=p, hp=hp, sl=sl:
                                      E.tensor_tensor(
                                kv_bf[hp:hp + 64, p, hp:hp + 64],
                                kv_acc[hp:hp + 64, p, :],
                                psB[0:64, sl, 0:64], ALU.add), s="V")
                        vkv_epi[sl] = vv
                vkv_done = k.val("V")

                for gi in range(4):
                    g = sq * 4 + gi
                    slot = g % 2
                    vx = load_xgrp(X0, g, slot)
                    vy = ln_apply(xg[:, slot], y_bf, negmu, rr, gi * 4, vx,
                                  yT_T[0])
                    vyT, yT_T[0] = transpose_grp(y_bf, yTg, psT, vy)
                    vq = [0] * NPAIR
                    for p in range(NPAIR):
                        sl = par_proj[0]; par_proj[0] ^= 1
                        k.wait("tensor", "V", epi[sl])
                        k.wait("tensor", "V", vyT)
                        for kc in range(KC):
                            k.op("tensor", lambda E, kc=kc, p=p, sl=sl: E.matmul(
                                psA[:, sl, :],
                                wq_bf[:, kc, p * 128:(p + 1) * 128],
                                yTg[:, kc, :], start=(kc == 0), stop=False))
                        vt = k.op("tensor", lambda E, p=p, sl=sl: E.matmul(
                            psA[:, sl, :], bq_r[0:1, p * 128:(p + 1) * 128],
                            ones_bf[0:1, :], start=False, stop=True), s="T")
                        k.wait("vector", "T", vt)
                        k.op("vector", lambda E, sl=sl: E.tensor_scalar(
                            scr1[:], psA[:, sl, :], 0.0, None, ALU.max))
                        vv = k.op("vector", lambda E, sl=sl: E.tensor_scalar(
                            scr2[:], psA[:, sl, :], 0.0, None, ALU.min), s="V")
                        k.wait("scalar", "V", vv)
                        vs = k.op("scalar", lambda E: E.activation(
                            scr2[:], scr2[:], AF.Exp), s="S")
                        k.wait("vector", "S", vs)
                        vv = k.op("vector", lambda E, p=p: E.tensor_tensor(
                            q_bf[:, p, :], scr1[:], scr2[:], ALU.add), s="V")
                        epi[sl] = vv
                        vq[p] = vv
                    vwp_epi = epi
                    vatt_cp = k.val("V")
                    for t in range(4):
                        k.wait("tensor", "V", vq[NPAIR - 1])
                        k.wait("tensor", "V", vkv_done)
                        k.wait("tensor", "V", vatt_cp)
                        vt = 0
                        for p in range(NPAIR):
                            bank, off = (p * 128) // 512, (p * 128) % 512
                            vt = k.op("tensor", lambda E, p=p, t=t,
                                      bank=bank, off=off: E.matmul(
                                psB[:, bank, off:off + 128],
                                q_bf[:, p, t * 128:(t + 1) * 128],
                                kv_bf[:, p, :],
                                start=True, stop=True),
                                s="T" if p == NPAIR - 1 else None)
                        k.wait("vector", "T", vt)
                        for bb in range(2):
                            vatt_cp = k.op("vector", lambda E, bb=bb: E.tensor_copy(
                                att_bf[:, bb * 512:(bb + 1) * 512],
                                psB[:, bb, :]), s="V")
                        vv0 = k.val("V")
                        for fh in range(2):
                            k.wait("tensor", "V", vatt_cp if fh == 0 else vv0)
                            vtt = 0
                            for f4 in range(4):
                                f = fh * 4 + f4
                                vtt = k.op("tensor", lambda E, f=f, f4=f4: E.transpose(
                                    psT[:, f4, :], att_bf[:, f * 128:(f + 1) * 128],
                                    identB[:]), s="T" if f4 == 3 else None)
                            k.wait("vector", "T", vtt)
                            vv0 = k.op("vector", lambda E, fh=fh: E.tensor_copy(
                                attT_bf[:, fh * 4:(fh + 1) * 4, :], psT[:, :, :]),
                                s="V")
                        for hcol in range(2):
                            sl = par_proj[0]; par_proj[0] ^= 1
                            k.wait("tensor", "V", epi[sl])
                            k.wait("tensor", "V", vv0)
                            for kc in range(KC):
                                k.op("tensor", lambda E, kc=kc, sl=sl, hcol=hcol:
                                     E.matmul(psA[:, sl, :], attT_bf[:, kc, :],
                                              wp_bf[:, kc, hcol * 512:(hcol + 1) * 512],
                                              start=(kc == 0), stop=False))
                            vt = k.op("tensor", lambda E, sl=sl, hcol=hcol: E.matmul(
                                psA[:, sl, :], ones_bf[0:1, 0:128],
                                bp_r[0:1, hcol * 512:(hcol + 1) * 512],
                                start=False, stop=True), s="T")
                            k.wait("vector", "T", vt)
                            k.wait("vector", "st", x1_free[slot])
                            vv = k.op("vector", lambda E, sl=sl, t=t, hcol=hcol,
                                      slot=slot: E.scalar_tensor_tensor(
                                x1sb[:, slot, t, hcol * 512:(hcol + 1) * 512],
                                psA[:, sl, :], 1.0,
                                xg[:, slot, t, hcol * 512:(hcol + 1) * 512],
                                ALU.mult, ALU.add), s="V")
                            epi[sl] = vv
                    xg_free[slot] = k.val("V")
                    k.wait("sync", "V", k.val("V"))
                    x1_free[slot] = k.dma("sync", grp_view(xdst(1), g),
                                          x1sb[:, slot, :, :], "st")
            k.flush(blk)

        if STAGE == 1:
            k.wait("sync", "st", k.val("st"))
            k.flush(blk)
            return nc

        k.barrier()
        with ExitStack() as ph:
            P = ph.enter_context
            w1_bf = P(nc.sbuf_tensor("w1_bf", [128, KC, H_FF], BF16))
            w2_bf = P(nc.sbuf_tensor("w2_bf", [128, HC_FF, D], BF16))
            xfT = P(nc.sbuf_tensor("xfT", [128, KC, 512], BF16))
            x1g = P(nc.sbuf_tensor("x1g", [128, 4, D], F32))
            y32 = P(nc.sbuf_tensor("y32", [128, 4, D], F32))
            h_sb = P(nc.sbuf_tensor("h_sb", [128, HC_FF, 512], F32))
            gel = P(nc.sbuf_tensor("gel", [128, HC_FF, 512], BF16))
            out_acc = P(nc.sbuf_tensor("out_acc", [128, 4, D], F32))
            scrB = P(nc.sbuf_tensor("scrB", [128, D], F32))
            sqscr = P(nc.sbuf_tensor("sqscr2", [128, D], BF16))
            gw_bf = P(nc.sbuf_tensor("gw_bf", [128, KC, E_EXP], BF16))
            b2_sb = P(nc.sbuf_tensor("b2_sb", [8, D], F32))
            logits = P(nc.sbuf_tensor("logits", [128, 4, E_EXP], F32))
            comb = P(nc.sbuf_tensor("comb", [128, 4, E_EXP], F32))
            combT = P(nc.sbuf_tensor("combT", [8, 4, 128], F32))
            ge1 = P(nc.sbuf_tensor("ge1", [128, E_EXP], F32))
            ge2 = P(nc.sbuf_tensor("ge2", [128, E_EXP], F32))
            gtmp = P(nc.sbuf_tensor("gtmp", [128, E_EXP], F32))
            m1_4 = P(nc.sbuf_tensor("m1_4", [128, 4], F32))
            m2_4 = P(nc.sbuf_tensor("m2_4", [128, 4], F32))
            d_4 = P(nc.sbuf_tensor("d_4", [128, 4], F32))
            c1_4 = P(nc.sbuf_tensor("c1_4", [128, 4], F32))
            c2_4 = P(nc.sbuf_tensor("c2_4", [128, 4], F32))
            lg1c = P(nc.sbuf_tensor("lg1c", [128, E_EXP, HC_FF], F32))
            lb1c = P(nc.sbuf_tensor("lb1c", [128, E_EXP, HC_FF], F32))
            stage16 = P(nc.sbuf_tensor("stage16", [16, 128], F32))
            b1cur = P(nc.sbuf_tensor("b1cur", [1, H_FF], BF16))
            wbar_bf = P(nc.sbuf_tensor("wbar_bf", [128, KC, E_EXP], BF16))
            wbar32 = P(nc.sbuf_tensor("wbar32", [128, KC], F32))
            sb1f = P(nc.sbuf_tensor("sb1f", [1, E_EXP], F32))
            sb1_bf = P(nc.sbuf_tensor("sb1_bf", [1, E_EXP], BF16))
            murow = P(nc.sbuf_tensor("murow", [1, 512], F32))
            vrow = P(nc.sbuf_tensor("vrow", [1, 512], F32))
            musqr = P(nc.sbuf_tensor("musqr", [1, 512], F32))
            rrow = P(nc.sbuf_tensor("rrow", [1, 512], F32))
            vbrow = P(nc.sbuf_tensor("vbrow", [1, 512], F32))
            lngbc = P(nc.sbuf_tensor("lngbc", [128, D], F32))
            lnbbc = P(nc.sbuf_tensor("lnbbc", [128, D], F32))
            negmu0 = P(nc.sbuf_tensor("negmu0", [128, 4], F32))
            rr0 = P(nc.sbuf_tensor("rr0", [128, 4], F32))
            negmu2 = P(nc.sbuf_tensor("negmu2", [128, 4], F32))
            rr2 = P(nc.sbuf_tensor("rr2", [128, 4], F32))
            st8 = P(nc.sbuf_tensor("st8b", [128, 8], F32))
            veps4 = P(nc.sbuf_tensor("veps4b", [128, 4], F32))
            psF1 = P(nc.psum_tensor("psF1", [128, 2, 512], F32))
            psSt = P(nc.psum_tensor("psSt", [128, 2, 512], F32))
            psF2 = P(nc.psum_tensor("psF2", [128, 4, 512], F32))

            k.dma("sync", b2_sb[:, :], B2_d[:, :], "xd")
            k.dma("gpsimd", gw_bf[:, :, :],
                  GW_d.rearrange("(a p) n -> p a n", p=128), "wd")
            vcp = 0
            for ei in range(E_EXP):
                for src_d, dst in ((LG1_d, lg1c), (LB1_d, lb1c)):
                    vld = k.dma("sync", stage16[:, :],
                                src_d[ei, :].rearrange("(a p) -> a p", a=16), "xd")
                    k.wait("tensor", "xd", vld)
                    k.wait("tensor", "V", vcp)
                    vt = k.op("tensor", lambda E: E.transpose(
                        psSt[:, 0, 0:16], stage16[:, :], identF[0:16, 0:16]),
                        s="T")
                    k.wait("vector", "T", vt)
                    vcp = k.op("vector", lambda E, ei=ei, dst=dst: E.tensor_copy(
                        dst[:, ei, :], psSt[:, 0, 0:16]), s="V")
                    k.wait("sync", "V", vcp)
            k.wait("tensor", "V", vcp)
            for src_d, dst in ((LG_d, lngbc), (LB_d, lnbbc)):
                k.wait("sync", "V", vcp)
                vrow_ld = k.dma("sync", scrB[0:1, 0:D], src_d[None, :], "xd")
                k.wait("tensor", "xd", vrow_ld)
                for c in range(2):
                    vt = k.op("tensor", lambda E, c=c: E.matmul(
                        psSt[:, 1, :], ones_f[0:1, :],
                        scrB[0:1, c * 512:(c + 1) * 512],
                        start=True, stop=True), s="T")
                    k.wait("vector", "T", vt)
                    vcp = k.op("vector", lambda E, dst=dst, c=c: E.tensor_copy(
                        dst[:, c * 512:(c + 1) * 512], psSt[:, 1, :]), s="V")
                    k.wait("tensor", "V", vcp)

            NGRP = NT // 4
            _ne = int(os.environ.get("K_NE", str(E_EXP)))
            _ng = int(os.environ.get("K_NG", str(NGRP)))
            pairs = [(g, e) for g in range(_ng) for e in range(_ne)]

            def load_w1(i, vT_req, vV_req=0):
                g, e = pairs[i]
                k.wait("gpsimd", "T", vT_req)
                k.wait("gpsimd", "V", vV_req)
                k.dma("gpsimd", w1_bf[:, 0:4, :],
                      W1_d[e, 0:512, :].rearrange("(a p) n -> p a n", p=128), "wd")
                return k.dma("gpsimd", w1_bf[:, 4:8, :],
                             W1_d[e, 512:1024, :].rearrange("(a p) n -> p a n",
                                                            p=128), "wd")

            def load_w2(i, vT_req):
                g, e = pairs[i]
                k.wait("gpsimd", "T", vT_req)
                k.dma("gpsimd", w2_bf[:, 0:8, :],
                      W2_d[e, 0:1024, :].rearrange("(a p) n -> p a n", p=128), "wd")
                return k.dma("gpsimd", w2_bf[:, 8:16, :],
                             W2_d[e, 1024:2048, :].rearrange("(a p) n -> p a n",
                                                             p=128), "wd")

            def load_b1(i, vT_req, vV_req):
                g, e = pairs[i]
                k.wait("gpsimd", "T", vT_req)
                k.wait("gpsimd", "V", vV_req)
                return k.dma("gpsimd", b1cur[0:1, :], B1_d[e:e + 1, :], "wd")

            if pairs:
                vW1 = load_w1(0, 0)
                vW2 = load_w2(0, 0)
                vB1 = load_b1(0, 0, 0)
            else:
                vW1 = vW2 = vB1 = 0
            wbar_done = [0] * E_EXP
            sb1_done = [0] * E_EXP
            psF2_free = [0, 0, 0, 0]
            vap_last = [0]
            x1g_free = 0
            x2_free = 0
            vxfT = 0
            vT_xfT_users = 0
            vfinS = 0

            for i, (g, e) in enumerate(pairs):
                gg = g
                if e == 0:
                    k.wait("sync", "V", x1g_free)
                    k.wait("sync", "st", x2_free)
                    vx = k.dma("sync", x1g[:, :, :], grp_view(Xs[1], gg), "xd")
                    ln_stats(x1g, vx, negmu0, rr0, 0, st8, sqscr, veps4)
                    k.wait("vector", "xd", vx)
                    k.wait("vector", "S", vfinS)
                    k.wait("vector", "T", vT_xfT_users)
                    vy = 0
                    for t in range(4):
                        vy = k.op("vector", lambda E, t=t: E.tensor_scalar(
                            y32[:, t, :], x1g[:, t, :], negmu0[:, t:t + 1],
                            rr0[:, t:t + 1], ALU.add, ALU.mult), s="V")
                    k.wait("tensor", "V", vy)
                    vv = k.val("V")
                    for t in range(4):
                        for fh in range(2):
                            k.wait("tensor", "V", vv)
                            sl2 = fh % 2
                            vt = 0
                            for f4 in range(4):
                                f = fh * 4 + f4
                                vt = k.op("tensor", lambda E, t=t, f=f, f4=f4,
                                          sl2=sl2: E.transpose(
                                    psSt[:, sl2, f4 * 128:(f4 + 1) * 128],
                                    y32[:, t, f * 128:(f + 1) * 128],
                                    identF[:]), s="T" if f4 == 3 else None)
                            k.wait("vector", "T", vt)
                            vv = k.op("vector", lambda E, t=t, fh=fh, sl2=sl2:
                                      E.tensor_copy(
                                xfT[:, fh * 4:(fh + 1) * 4,
                                    t * 128:(t + 1) * 128],
                                psSt[:, sl2, :].rearrange(
                                    "p (a b) -> p a b", a=4)), s="V")
                    vxfT = vv
                    if os.environ.get("K_P2") == "ln":
                        continue
                    vgate = 0
                    for t in range(4):
                        k.wait("tensor", "V", vxfT)
                        k.wait("tensor", "V", vgate)
                        for kc in range(KC):
                            vt = k.op("tensor", lambda E, kc=kc, t=t: E.matmul(
                                psSt[:, 0, 0:E_EXP],
                                xfT[:, kc, t * 128:(t + 1) * 128],
                                gw_bf[:, kc, :], start=(kc == 0),
                                stop=(kc == KC - 1)),
                                s="T" if kc == KC - 1 else None)
                        if os.environ.get("K_P2") == "gmm0":
                            continue
                        k.wait("vector", "T", vt)
                        vgate = k.op("vector", lambda E, t=t: E.tensor_copy(
                            logits[:, t, :], psSt[:, 0, 0:E_EXP]), s="V")
                        if os.environ.get("K_P2") == "gmm":
                            continue
                        k.wait("vector", "V", vgate)
                        vgate = k.op("vector", lambda E, t=t: E.reduce_max(
                            m1_4[:, t:t + 1], logits[:, t, :], AX), s="V")
                        k.wait("vector", "V", vgate)
                        vgate = k.op("vector", lambda E, t=t: E.tensor_scalar(
                            ge1[:, :], logits[:, t, :], m1_4[:, t:t + 1], None,
                            ALU.is_equal), s="V")
                        k.wait("vector", "V", vgate)
                        vgate = k.op("vector", lambda E, t=t: E.scalar_tensor_tensor(
                            ge2[:, :], ge1[:, :], -1.0e30, logits[:, t, :],
                            ALU.mult, ALU.add), s="V")
                        k.wait("vector", "V", vgate)
                        vgate = k.op("vector", lambda E, t=t: E.reduce_max(
                            m2_4[:, t:t + 1], ge2[:, :], AX), s="V")
                        k.wait("vector", "V", vgate)
                    if os.environ.get("K_P2") in ("gate", "gmm", "gmm0"):
                        continue
                    vgate = k.op("vector", lambda E: E.tensor_tensor(
                        d_4[:, :], m1_4[:, :], m2_4[:, :], ALU.subtract), s="V")
                    k.wait("scalar", "V", vgate)
                    vsg = k.op("scalar", lambda E: E.activation(
                        c1_4[:, :], d_4[:, :], AF.Sigmoid), s="S")
                    k.wait("vector", "S", vsg)
                    vcmb = k.op("vector", lambda E: E.tensor_scalar(
                        c2_4[:, :], c1_4[:, :], -1.0, 1.0, ALU.mult, ALU.add),
                        s="V")
                    k.wait("vector", "V", vcmb)
                    for t in range(4):
                        vcmb = k.op("vector", lambda E, t=t: E.tensor_scalar(
                            ge1[:, :], logits[:, t, :], m1_4[:, t:t + 1], None,
                            ALU.is_equal), s="V")
                        k.op("vector", lambda E, t=t: E.tensor_scalar(
                            ge2[:, :], logits[:, t, :], m2_4[:, t:t + 1], None,
                            ALU.is_equal))
                        k.wait("vector", "V", vcmb)
                        vcmb = k.op("vector", lambda E, t=t: E.tensor_scalar(
                            gtmp[:, :], ge1[:, :], c1_4[:, t:t + 1], None,
                            ALU.mult), s="V")
                        k.wait("vector", "V", vcmb)
                        vcmb = k.op("vector", lambda E, t=t: E.scalar_tensor_tensor(
                            comb[:, t, :], ge2[:, :], c2_4[:, t:t + 1],
                            gtmp[:, :], ALU.mult, ALU.add), s="V")
                    if os.environ.get("K_P2") == "top2":
                        continue
                    for t in range(4):
                        k.wait("tensor", "V", vcmb)
                        vt = k.op("tensor", lambda E, t=t: E.transpose(
                            psSt[0:8, 0, 0:128], comb[:, t, :], identF[:]), s="T")
                        k.wait("vector", "T", vt)
                        vct = k.op("vector", lambda E, t=t: E.tensor_copy(
                            combT[:, t, :], psSt[0:8, 0, 0:128]), s="V")
                        for c in range(2):
                            k.wait("tensor", "V", max(vct, psF2_free[t]))
                            vt = k.op("tensor", lambda E, t=t, c=c: E.matmul(
                                psF2[:, t, :], combT[0:8, t, :],
                                b2_sb[0:8, c * 512:(c + 1) * 512],
                                start=True, stop=True), s="T")
                            k.wait("vector", "T", vt)
                            psF2_free[t] = k.op(
                                "vector", lambda E, t=t, c=c: E.tensor_copy(
                                    out_acc[:, t, c * 512:(c + 1) * 512],
                                    psF2[:, t, :]), s="V")

                if os.environ.get("K_P2") == "pre":
                    continue
                if g == 0:
                    k.wait("vector", "wd", vW1)
                    vwb = 0
                    for kc in range(KC):
                        vwb = k.op("vector", lambda E, kc=kc: E.reduce_sum(
                            wbar32[:, kc:kc + 1], w1_bf[:, kc, :], AX), s="V")
                    k.wait("vector", "V", vwb)
                    wbar_done[e] = k.op("vector", lambda E, e=e: E.tensor_copy(
                        wbar_bf[:, :, e], wbar32[:, :]), s="V")
                    k.wait("vector", "wd", vB1)
                    vsb = k.op("vector", lambda E, e=e: E.reduce_sum(
                        sb1f[0:1, e:e + 1], b1cur[0:1, :], AX), s="V")
                    k.wait("vector", "V", vsb)
                    sb1_done[e] = k.op("vector", lambda E, e=e: E.tensor_copy(
                        sb1_bf[0:1, e:e + 1], sb1f[0:1, e:e + 1]), s="V")

                vmm1 = [0] * HC_FF
                vevac = [0] * HC_FF
                vsq = [0] * HC_FF
                vsqmm = [0] * HC_FF
                k.wait("tensor", "wd", max(vW1, vB1))
                k.wait("tensor", "V", vxfT)

                def emit_mm1(hc):
                    sl = hc % 2
                    if hc >= 2:
                        k.wait("tensor", "S", vsq[hc - 2])
                    for kc in range(KC):
                        k.op("tensor", lambda E, kc=kc, sl=sl, hc=hc: E.matmul(
                            psF1[:, sl, :],
                            w1_bf[:, kc, hc * 128:(hc + 1) * 128],
                            xfT[:, kc, :], start=(kc == 0), stop=False))
                    vmm1[hc] = k.op("tensor", lambda E, sl=sl, hc=hc: E.matmul(
                        psF1[:, sl, :], b1cur[0:1, hc * 128:(hc + 1) * 128],
                        ones_bf[0:1, :], start=False, stop=True), s="T")

                def emit_sq_s(hc):
                    sl = hc % 2
                    k.wait("scalar", "T", vmm1[hc])
                    if hc >= 2:
                        k.wait("scalar", "T", vsqmm[hc - 2])
                    vevac[hc] = k.op("scalar", lambda E, hc=hc, sl=sl:
                                     E.activation(h_sb[:, hc, :], psF1[:, sl, :],
                                                  AF.Copy), s="S")
                    vsq[hc] = k.op("scalar", lambda E, hc=hc, sl=sl: E.activation(
                        sqscr[:, sl * 512:(sl + 1) * 512], h_sb[:, hc, :],
                        AF.Square), s="S")

                def emit_sqmm(hc):
                    sl = hc % 2
                    k.wait("tensor", "S", vsq[hc])
                    if hc == 0:
                        k.wait("tensor", "V", vap_last[0])
                    vsqmm[hc] = k.op("tensor", lambda E, hc=hc, sl=sl: E.matmul(
                        psSt[0:1, 1, :], ones_col[:, :],
                        sqscr[:, sl * 512:(sl + 1) * 512],
                        start=(hc == 0), stop=(hc == HC_FF - 1)), s="T")

                emit_mm1(0)
                for hc in range(1, HC_FF):
                    emit_mm1(hc)
                    emit_sq_s(hc - 1)
                    emit_sqmm(hc - 1)
                emit_sq_s(HC_FF - 1)
                emit_sqmm(HC_FF - 1)
                k.wait("tensor", "V", max(wbar_done[e], sb1_done[e]))
                for kc in range(KC):
                    k.op("tensor", lambda E, kc=kc, e=e: E.matmul(
                        psSt[0:1, 0, :], wbar_bf[:, kc, e:e + 1], xfT[:, kc, :],
                        start=(kc == 0), stop=False))
                vmu = k.op("tensor", lambda E, e=e: E.matmul(
                    psSt[0:1, 0, :], sb1_bf[0:1, e:e + 1], ones_bf[0:1, :],
                    start=False, stop=True), s="T")
                k.wait("vector", "T", max(vmu, vsqmm[HC_FF - 1]))
                vrw = k.op("vector", lambda E: E.tensor_scalar(
                    murow[0:1, :], psSt[0:1, 0, :], 1.0 / H_FF, None, ALU.mult),
                    s="V")
                k.wait("vector", "V", vrw)
                k.op("vector", lambda E: E.tensor_tensor(
                    musqr[0:1, :], murow[0:1, :], murow[0:1, :], ALU.mult))
                vrw = k.op("vector", lambda E: E.tensor_scalar(
                    vrow[0:1, :], psSt[0:1, 1, :], 1.0 / H_FF, None, ALU.mult),
                    s="V")
                k.wait("vector", "V", vrw)
                vrw = k.op("vector", lambda E: E.scalar_tensor_tensor(
                    rrow[0:1, :], vrow[0:1, :], 1.0, musqr[0:1, :],
                    ALU.mult, ALU.subtract), s="V")
                k.wait("vector", "V", vrw)
                vrw = k.op("vector", lambda E: E.tensor_scalar(
                    vrow[0:1, :], rrow[0:1, :], EPS, None, ALU.add), s="V")
                k.wait("scalar", "V", vrw)
                vsd = k.op("scalar", lambda E: E.activation(
                    musqr[0:1, :], vrow[0:1, :], AF.Sqrt), s="S")
                k.wait("vector", "S", vsd)
                vrw = k.op("vector", lambda E: E.reciprocal(
                    rrow[0:1, :], musqr[0:1, :]), s="V")
                k.wait("vector", "V", vrw)
                vrw = k.op("vector", lambda E: E.tensor_tensor(
                    vbrow[0:1, :], murow[0:1, :], rrow[0:1, :], ALU.mult), s="V")
                k.wait("tensor", "V", vrw)
                k.op("tensor", lambda E: E.matmul(
                    psSt[:, 0, :], ones_f[0:1, :], rrow[0:1, :],
                    start=True, stop=True))
                vbc = k.op("tensor", lambda E: E.matmul(
                    psSt[:, 1, :], ones_f[0:1, :], vbrow[0:1, :],
                    start=True, stop=True), s="T")
                vgel = [0] * HC_FF
                k.wait("vector", "T", vbc)
                for hc in range(HC_FF):
                    k.wait("vector", "S", vevac[hc])
                    k.op("vector", lambda E, hc=hc: E.tensor_tensor(
                        h_sb[:, hc, :], h_sb[:, hc, :], psSt[:, 0, :], ALU.mult))
                    vap = k.op("vector", lambda E, hc=hc: E.tensor_tensor(
                        h_sb[:, hc, :], h_sb[:, hc, :], psSt[:, 1, :],
                        ALU.subtract), s="V")
                    k.wait("scalar", "V", vap)
                    vgel[hc] = k.op("scalar", lambda E, hc=hc, e=e: E.activation(
                        gel[:, hc, :], h_sb[:, hc, :], AF.Gelu,
                        scale=lg1c[:, e, hc:hc + 1],
                        bias=lb1c[:, e, hc:hc + 1]), s="S")
                    vap_last[0] = vap
                if i + 1 < len(pairs):
                    vW1 = load_w1(i + 1, vmm1[HC_FF - 1],
                                  wbar_done[e] if g == 0 else 0)
                    vB1 = load_b1(i + 1, vmm1[HC_FF - 1],
                                  sb1_done[e] if g == 0 else 0)
                k.wait("tensor", "wd", vW2)
                vmm2 = [[0] * 4, [0] * 4]
                for c in range(2):
                    for hc in range(HC_FF):
                        k.wait("tensor", "S", vgel[hc])
                        if hc == 0:
                            for t in range(4):
                                k.wait("tensor", "V", psF2_free[t])
                        for t in range(4):
                            vmm2[c][t] = k.op(
                                "tensor", lambda E, c=c, hc=hc, t=t: E.matmul(
                                    psF2[:, t, :],
                                    gel[:, hc, t * 128:(t + 1) * 128],
                                    w2_bf[:, hc, c * 512:(c + 1) * 512],
                                    start=(hc == 0), stop=(hc == HC_FF - 1)),
                                s="T" if hc == HC_FF - 1 else None)
                    for t in range(4):
                        k.wait("vector", "T", vmm2[c][t])
                        psF2_free[t] = k.op(
                            "vector", lambda E, c=c, t=t, e=e:
                            E.scalar_tensor_tensor(
                                out_acc[:, t, c * 512:(c + 1) * 512],
                                psF2[:, t, :], comb[:, t, e:e + 1],
                                out_acc[:, t, c * 512:(c + 1) * 512],
                                ALU.mult, ALU.add), s="V")
                vT_xfT_users = max(vmm1[HC_FF - 1], vmu)
                if i + 1 < len(pairs):
                    vW2 = load_w2(i + 1, vmm2[1][3])

                if e == E_EXP - 1:
                    for t in range(4):
                        k.op("vector", lambda E, t=t: E.tensor_scalar(
                            y32[:, t, :], x1g[:, t, :], negmu0[:, t:t + 1],
                            rr0[:, t:t + 1], ALU.add, ALU.mult))
                        vfy = k.op("vector", lambda E, t=t: E.tensor_tensor(
                            y32[:, t, :], y32[:, t, :], out_acc[:, t, :],
                            ALU.add), s="V")
                        k.op("vector", lambda E, t=t: E.reduce_sum(
                            st8[:, t:t + 1], y32[:, t, :], AX))
                        k.wait("scalar", "V", vfy)
                        vfinS = k.op("scalar", lambda E, t=t: E.activation(
                            sqscr[:], y32[:, t, :], AF.Square,
                            accum_out=st8[:, 4 + t:5 + t]), s="S")
                    k.wait("vector", "S", vfinS)
                    vfv = k.op("vector", lambda E: E.tensor_scalar(
                        negmu2[:], st8[:, 0:4], -1.0 / D, None, ALU.mult), s="V")
                    k.wait("vector", "V", vfv)
                    vfv = k.op("vector", lambda E: E.tensor_tensor(
                        veps4[:], negmu2[:], negmu2[:], ALU.mult), s="V")
                    k.wait("vector", "V", vfv)
                    vfv = k.op("vector", lambda E: E.scalar_tensor_tensor(
                        rr2[:], st8[:, 4:8], 1.0 / D, veps4[:],
                        ALU.mult, ALU.subtract), s="V")
                    k.wait("vector", "V", vfv)
                    vfv = k.op("vector", lambda E: E.tensor_scalar(
                        veps4[:], rr2[:], EPS, None, ALU.add), s="V")
                    k.wait("scalar", "V", vfv)
                    vfs = k.op("scalar", lambda E: E.activation(
                        veps4[:], veps4[:], AF.Sqrt), s="S")
                    k.wait("vector", "S", vfs)
                    vfv = k.op("vector", lambda E: E.reciprocal(
                        rr2[:], veps4[:]), s="V")
                    k.wait("vector", "V", vfv)
                    k.wait("vector", "st", x2_free)
                    vst2 = 0
                    for t in range(4):
                        k.op("vector", lambda E, t=t: E.tensor_scalar(
                            scrB[:, :], y32[:, t, :], negmu2[:, t:t + 1],
                            rr2[:, t:t + 1], ALU.add, ALU.mult))
                        k.op("vector", lambda E: E.scalar_tensor_tensor(
                            scrB[:, :], scrB[:, :], 1.0, lngbc[:, :],
                            ALU.mult, ALU.mult))
                        k.op("vector", lambda E: E.tensor_tensor(
                            scrB[:, :], scrB[:, :], lnbbc[:, :], ALU.add))
                        vst2 = k.op("vector", lambda E, t=t: E.tensor_tensor(
                            x1g[:, t, :], scrB[:, :], x1g[:, t, :], ALU.add),
                            s="V")
                    x1g_free = vst2
                    k.wait("sync", "V", vst2)
                    x2_free = k.dma("sync", grp_view(xdst(2), gg),
                                    x1g[:, :, :], "st")
            k.flush(blk)

        if STAGE == 2:
            k.wait("sync", "st", k.val("st"))
            k.flush(blk)
            return nc

        k.barrier()
        with ExitStack() as ph:
            P = ph.enter_context
            WKT = P(nc.sbuf_tensor("WKT", [128, KC, D], BF16))
            WQT = P(nc.sbuf_tensor("WQT", [128, KC, D], BF16))
            WVT = P(nc.sbuf_tensor("WVT", [128, KC, D], BF16))
            WOT = P(nc.sbuf_tensor("WOT", [128, KC, D], BF16))
            yT = P(nc.sbuf_tensor("yT3", [128, KC, SEQ], BF16))
            zT_all = P(nc.sbuf_tensor("zT_all", [128, NPAIR, SEQ], BF16))
            sh = P(nc.sbuf_tensor("sh", [128, 4096], F32))
            KfmA = P(nc.sbuf_tensor("KfmA", [128, SEQ], BF16))
            KfmB = P(nc.sbuf_tensor("KfmB", [128, SEQ], BF16))
            Qfm = P(nc.sbuf_tensor("Qfm", [128, SEQ], BF16))
            v_tm = P(nc.sbuf_tensor("v_tm", [128, 16, 128], BF16))
            P_bf = P(nc.sbuf_tensor("P_bf", [128, 2, SEQ], BF16))
            Vp_bf = P(nc.sbuf_tensor("Vp_bf", [128, 2, HD], BF16))
            zacc = P(nc.sbuf_tensor("zacc", [128, SEQ], F32))
            masks = P(nc.sbuf_tensor("masks", [128, 4, 512], BF16))
            x2g = P(nc.sbuf_tensor("x2g", [128, 4, D], F32))
            mx8 = P(nc.sbuf_tensor("mx8", [128, 8], F32))
            mraw = P(nc.sbuf_tensor("mraw", [128, 2], F32))
            elli = P(nc.sbuf_tensor("elli", [128, 2], F32))
            mnegh = P(nc.sbuf_tensor("mnegh", [128, 2], F32))
            ellh = P(nc.sbuf_tensor("ellh", [128, 2], F32))
            negmu3 = P(nc.sbuf_tensor("negmu3", [128, 16], F32))
            rr3 = P(nc.sbuf_tensor("rr3", [128, 16], F32))
            st8 = P(nc.sbuf_tensor("st8c", [128, 8], F32))
            veps4 = P(nc.sbuf_tensor("veps4c", [128, 4], F32))
            sqscr = P(nc.sbuf_tensor("sqscr3", [128, D], BF16))
            psS = P(nc.psum_tensor("psS", [128, 4, 512], F32))
            psZ = P(nc.psum_tensor("psZ", [128, 4, 512], F32))

            def y32v(t):
                return sh[:, t * D:(t + 1) * D]

            def s2v(head, a, b):
                return sh[:, head * SEQ + a: head * SEQ + b]

            vfm = sh[:, 0:SEQ]
            wstage = sh[:, 0:D]

            k.op("gpsimd", lambda E: E.memset(KfmA[:], 0.0), s="G")
            k.op("gpsimd", lambda E: E.memset(KfmB[:], 0.0), s="G")
            k.op("gpsimd", lambda E: E.memset(masks[:], 0.0), s="G")
            k.wait("gpsimd", "G", k.val("G"))
            vmask = 0
            for kk in range(4):
                vmask = k.op("gpsimd", lambda E, kk=kk: E.affine_select(
                    masks[:, kk, :], masks[:, kk, :], [[-1, 512]], ALU.is_ge,
                    -1.0e30, base=128 * kk, channel_multiplier=1), s="G")

            vwt = 0
            for W_dr, WT in ((WK_d, WKT), (WQ_d, WQT), (WV_d, WVT), (WO_d, WOT)):
                for rc in range(KC):
                    k.wait("sync", "V", vwt)
                    vld = k.dma("sync", wstage[:, :],
                                W_dr[rc * 128:(rc + 1) * 128, :], "xd")
                    k.wait("tensor", "xd", vld)
                    for cb in range(2):
                        k.wait("tensor", "V", vwt)
                        vt = 0
                        for f4 in range(4):
                            cc = cb * 4 + f4
                            vt = k.op("tensor", lambda E, cc=cc, f4=f4, cb=cb:
                                      E.transpose(
                                psS[:, cb, f4 * 128:(f4 + 1) * 128],
                                wstage[:, cc * 128:(cc + 1) * 128],
                                identF[:]), s="T" if f4 == 3 else None)
                        k.wait("vector", "T", vt)
                        vwt = k.op("vector", lambda E, WT=WT, rc=rc, cb=cb:
                                   E.tensor_copy(
                            WT[:, cb * 4:(cb + 1) * 4, rc * 128:(rc + 1) * 128],
                            psS[:, cb, :].rearrange("p (a b) -> p a b", a=4)),
                            s="V")

            x2g_free = 0
            x3_free = 0
            psS_free = [0, 0, 0, 0]
            psS_freeS = [0, 0, 0, 0]
            zfree = [0, 0, 0, 0]
            vz_last = 0
            vexp_last = 0
            vtm_done = 0

            for sq in range(BPC):
                for gi in range(4):
                    g = sq * 4 + gi
                    k.wait("sync", "V", x2g_free)
                    k.wait("sync", "st", x3_free)
                    vx = k.dma("sync", x2g[:, :, :], grp_view(Xs[2], g), "xd")
                    ln_stats(x2g, vx, negmu3, rr3, gi * 4, st8, sqscr, veps4)
                    k.wait("vector", "xd", vx)
                    k.wait("vector", "S", k.val("S"))
                    k.wait("vector", "T", vz_last)
                    vy = 0
                    for t in range(4):
                        vy = k.op("vector", lambda E, t=t, gi=gi: E.tensor_scalar(
                            y32v(t), x2g[:, t, :], negmu3[:, gi * 4 + t:gi * 4 + t + 1],
                            rr3[:, gi * 4 + t:gi * 4 + t + 1],
                            ALU.add, ALU.mult), s="V")
                    x2g_free = vy
                    k.wait("tensor", "V", vy)
                    vv = k.val("V")
                    for t in range(4):
                        for fh in range(2):
                            k.wait("tensor", "V", vv)
                            sl2 = fh % 2
                            vt = 0
                            for f4 in range(4):
                                f = fh * 4 + f4
                                vt = k.op("tensor", lambda E, t=t, f=f, f4=f4,
                                          sl2=sl2: E.transpose(
                                    psS[:, sl2, f4 * 128:(f4 + 1) * 128],
                                    y32v(t)[:, f * 128:(f + 1) * 128],
                                    identF[:]), s="T" if f4 == 3 else None)
                            k.wait("vector", "T", vt)
                            vv = k.op("vector", lambda E, t=t, fh=fh, gi=gi,
                                      sl2=sl2: E.tensor_copy(
                                yT[:, fh * 4:(fh + 1) * 4,
                                   gi * 512 + t * 128: gi * 512 + (t + 1) * 128],
                                psS[:, sl2, :].rearrange("p (a b) -> p a b", a=4)),
                                s="V")
                    psS_free = [vv, vv, vv, vv]

                for pr in range(NPAIR):
                    for (WT, which) in ((WKT, "K"), (WQT, "Q")):
                        for tc in range(4):
                            sl = tc % 2
                            k.wait("tensor", "V", psS_free[sl])
                            k.wait("tensor", "S", psS_freeS[sl])
                            for kc in range(KC):
                                vt = k.op("tensor", lambda E, kc=kc, sl=sl,
                                          tc=tc, pr=pr, WT=WT: E.matmul(
                                    psS[:, sl, :],
                                    WT[:, kc, pr * 128:(pr + 1) * 128],
                                    yT[:, kc, tc * 512:(tc + 1) * 512],
                                    start=(kc == 0), stop=(kc == KC - 1)),
                                    s="T" if kc == KC - 1 else None)
                            k.wait("vector", "T", vt)
                            if which == "Q":
                                psS_free[sl] = k.op(
                                    "vector", lambda E, sl=sl, tc=tc:
                                    E.tensor_copy(
                                        Qfm[:, tc * 512:(tc + 1) * 512],
                                        psS[:, sl, :]), s="V")
                            else:
                                k.op("vector", lambda E, sl=sl, tc=tc:
                                     E.tensor_copy(
                                    KfmA[0:64, tc * 512:(tc + 1) * 512],
                                    psS[0:64, sl, :]), s="V")
                                psS_free[sl] = k.op(
                                    "vector", lambda E, sl=sl, tc=tc:
                                    E.tensor_copy(
                                        KfmB[64:128, tc * 512:(tc + 1) * 512],
                                        psS[64:128, sl, :]), s="V")
                    k.wait("vector", "S", vexp_last)
                    for tc in range(4):
                        sl = tc % 2
                        k.wait("tensor", "V", psS_free[sl])
                        k.wait("tensor", "S", psS_freeS[sl])
                        for kc in range(KC):
                            vt = k.op("tensor", lambda E, kc=kc, sl=sl, tc=tc,
                                      pr=pr: E.matmul(
                                psS[:, sl, :],
                                WVT[:, kc, pr * 128:(pr + 1) * 128],
                                yT[:, kc, tc * 512:(tc + 1) * 512],
                                start=(kc == 0), stop=(kc == KC - 1)),
                                s="T" if kc == KC - 1 else None)
                        k.wait("vector", "T", vt)
                        psS_free[sl] = k.op(
                            "vector", lambda E, sl=sl, tc=tc: E.tensor_copy(
                                vfm[:, tc * 512:(tc + 1) * 512], psS[:, sl, :]),
                            s="V")
                    vvf = psS_free[1]
                    k.wait("tensor", "V", vvf)
                    for tb in range(4):
                        sl = tb % 2
                        k.wait("tensor", "V", psS_free[sl])
                        k.wait("tensor", "S", psS_freeS[sl])
                        vt = 0
                        for f4 in range(4):
                            ti = tb * 4 + f4
                            vt = k.op("tensor", lambda E, ti=ti, f4=f4, sl=sl:
                                      E.transpose(
                                psS[:, sl, f4 * 128:(f4 + 1) * 128],
                                vfm[:, ti * 128:(ti + 1) * 128],
                                identF[:]), s="T" if f4 == 3 else None)
                        k.wait("vector", "T", vt)
                        psS_free[sl] = k.op(
                            "vector", lambda E, tb=tb, sl=sl: E.tensor_copy(
                                v_tm[:, tb * 4:(tb + 1) * 4, :],
                                psS[:, sl, :].rearrange("p (a b) -> p a b", a=4)),
                            s="V")
                    vtm_done = psS_free[(4 - 1) % 2]
                    for ci in range(16):
                        nq = ci // 4 + 1
                        vdiag = 0
                        for q in range(nq):
                            diag = (q == ci // 4)
                            for head in range(2):
                                sl = head * 2 + (q % 2)
                                k.wait("tensor", "V", psS_free[sl])
                                k.wait("tensor", "S", psS_freeS[sl])
                                Kpad = KfmA if head == 0 else KfmB
                                vt = k.op("tensor", lambda E, head=head, sl=sl,
                                          ci=ci, q=q, Kpad=Kpad: E.matmul(
                                    psS[:, sl, :],
                                    Kpad[:, ci * 128:(ci + 1) * 128],
                                    Qfm[:, q * 512:(q + 1) * 512],
                                    start=True, stop=True), s="T")
                                if diag:
                                    k.wait("vector", "T", vt)
                                    k.wait("vector", "G", vmask)
                                    k.wait("vector", "S", vexp_last)
                                    psS_free[sl] = k.op(
                                        "vector", lambda E, head=head, sl=sl,
                                        ci=ci, q=q: E.tensor_tensor(
                                            s2v(head, q * 512, (q + 1) * 512),
                                            psS[:, sl, :], masks[:, ci % 4, :],
                                            ALU.add), s="V")
                                    vdiag = psS_free[sl]
                                else:
                                    k.wait("scalar", "T", vt)
                                    k.wait("scalar", "S", vexp_last)
                                    psS_freeS[sl] = k.op(
                                        "scalar", lambda E, head=head, sl=sl,
                                        ci=ci, q=q: E.activation(
                                            s2v(head, q * 512, (q + 1) * 512),
                                            psS[:, sl, :], AF.Copy), s="S")
                        vmx = 0
                        for head in range(2):
                            for q in range(nq):
                                k.wait("vector", "S", psS_freeS[head * 2 + q % 2])
                                k.wait("vector", "V", vdiag)
                                vmx = k.op("vector", lambda E, head=head, q=q:
                                           E.reduce_max(
                                    mx8[:, head * 4 + q:head * 4 + q + 1],
                                    s2v(head, q * 512, (q + 1) * 512), AX), s="V")
                        k.wait("vector", "V", vmx)
                        for head in range(2):
                            vmx = k.op("vector", lambda E, head=head, nq=nq:
                                       E.reduce_max(mraw[:, head:head + 1],
                                                    mx8[:, head * 4:head * 4 + nq],
                                                    AX), s="V")
                        k.wait("vector", "V", vmx)
                        vmx = k.op("vector", lambda E: E.tensor_scalar(
                            mnegh[:, :], mraw[:, :], -0.125, None, ALU.mult),
                            s="V")
                        k.wait("scalar", "V", vmx)
                        k.wait("scalar", "T", vz_last)
                        for head in range(2):
                            vexp_last = k.op("scalar", lambda E, head=head,
                                             nq=nq: E.activation(
                                P_bf[:, head, 0:512 * nq],
                                s2v(head, 0, 512 * nq), AF.Exp,
                                scale=0.125,
                                bias=mnegh[:, head:head + 1]), s="S")
                        k.wait("vector", "S", vexp_last)
                        k.wait("vector", "T", vz_last)
                        vVp = 0
                        for head in range(2):
                            vVp = k.op("vector", lambda E, head=head, nq=nq:
                                       E.reduce_sum(elli[:, head:head + 1],
                                                    P_bf[:, head, 0:512 * nq],
                                                    AX), s="V")
                        k.wait("vector", "V", vVp)
                        vVp = k.op("vector", lambda E: E.reciprocal(
                            ellh[:, :], elli[:, :]), s="V")
                        k.wait("vector", "V", vVp)
                        for head in range(2):
                            vVp = k.op("vector", lambda E, head=head, ci=ci:
                                       E.tensor_scalar(
                                Vp_bf[:, head, :],
                                v_tm[:, ci, head * 64:(head + 1) * 64],
                                ellh[:, head:head + 1], None, ALU.mult), s="V")
                        k.wait("tensor", "V", vVp)
                        for q in range(nq):
                            k.wait("tensor", "V", zfree[q])
                            for head in range(2):
                                vz_last = k.op("tensor", lambda E, head=head,
                                               q=q, ci=ci: E.matmul(
                                    psZ[head * 64:(head + 1) * 64, q, :],
                                    Vp_bf[:, head, :],
                                    P_bf[:, head, q * 512:(q + 1) * 512],
                                    start=True, stop=True,
                                    tile_position=(0, head * 64)), s="T")
                            k.wait("vector", "T", vz_last)
                            if ci == 4 * q:
                                zfree[q] = k.op("vector", lambda E, q=q:
                                                E.tensor_copy(
                                    zacc[:, q * 512:(q + 1) * 512],
                                    psZ[:, q, :]), s="V")
                            else:
                                zfree[q] = k.op("vector", lambda E, q=q:
                                                E.tensor_tensor(
                                    zacc[:, q * 512:(q + 1) * 512],
                                    zacc[:, q * 512:(q + 1) * 512],
                                    psZ[:, q, :], ALU.add), s="V")
                    vzT = k.op("vector", lambda E, pr=pr: E.tensor_copy(
                        zT_all[:, pr, :], zacc[:, :]), s="V")
                    zfree = [vzT] * 4

                k.wait("tensor", "V", vzT)
                for gi in range(4):
                    g = sq * 4 + gi
                    k.wait("sync", "V", x2g_free)
                    k.wait("sync", "st", x3_free)
                    vx = k.dma("sync", x2g[:, :, :], grp_view(Xs[2], g), "xd")
                    vres = 0
                    for t in range(4):
                        for c in range(2):
                            sl = (t * 2 + c) % 2
                            k.wait("tensor", "V", psS_free[sl])
                            k.wait("tensor", "S", psS_freeS[sl])
                            for fc in range(KC):
                                vt = k.op("tensor", lambda E, fc=fc, sl=sl,
                                          gi=gi, t=t, c=c: E.matmul(
                                    psS[:, sl, :], zT_all[:, fc,
                                        gi * 512 + t * 128:
                                        gi * 512 + (t + 1) * 128],
                                    WOT[:, fc, c * 512:(c + 1) * 512],
                                    start=(fc == 0), stop=(fc == KC - 1)),
                                    s="T" if fc == KC - 1 else None)
                            k.wait("vector", "T", vt)
                            k.wait("vector", "xd", vx)
                            psS_free[sl] = k.op(
                                "vector", lambda E, sl=sl, t=t, c=c:
                                E.scalar_tensor_tensor(
                                    x2g[:, t, c * 512:(c + 1) * 512],
                                    psS[:, sl, :], 1.0,
                                    x2g[:, t, c * 512:(c + 1) * 512],
                                    ALU.mult, ALU.add), s="V")
                            vres = psS_free[sl]
                    x2g_free = vres
                    k.wait("sync", "V", vres)
                    x3_free = k.dma("sync", grp_view(xdst(3), g),
                                    x2g[:, :, :], "st")
            k.flush(blk)

        if STAGE == 3:
            k.wait("sync", "st", k.val("st"))
            k.flush(blk)
            return nc

        k.barrier()
        with ExitStack() as ph:
            P = ph.enter_context
            Wf_bf = P(nc.sbuf_tensor("Wf_bf", [128, KC, D], BF16))
            A1_bf = P(nc.sbuf_tensor("A1_bf", [128, KC, D], BF16))
            A2_bf = P(nc.sbuf_tensor("A2_bf", [128, KC, D], BF16))
            bfr = P(nc.sbuf_tensor("bfr", [1, D], BF16))
            x3g = P(nc.sbuf_tensor("x3g", [128, 4, D], F32))
            y_bf = P(nc.sbuf_tensor("y_bf4", [128, 4, D], BF16))
            yTg = P(nc.sbuf_tensor("yTg4", [128, KC, 512], BF16))
            h1T = P(nc.sbuf_tensor("h1T", [128, KC, 512], BF16))
            h2T = P(nc.sbuf_tensor("h2T", [128, KC, 512], BF16))
            scrT = P(nc.sbuf_tensor("scrT", [128, 512], F32))
            negmu4 = P(nc.sbuf_tensor("negmu4", [128, 4], F32))
            rr4 = P(nc.sbuf_tensor("rr4", [128, 4], F32))
            st8 = P(nc.sbuf_tensor("st8d", [128, 8], F32))
            veps4 = P(nc.sbuf_tensor("veps4d", [128, 4], F32))
            sqscr = P(nc.sbuf_tensor("sqscr4", [128, D], BF16))
            psF = P(nc.psum_tensor("psF4", [128, 2, 512], F32))
            psT = P(nc.psum_tensor("psT4", [128, 4, 128], BF16))

            vW = 0
            for w_sb, w_dr in ((Wf_bf, WF_d), (A1_bf, A1_d), (A2_bf, A2_d)):
                vW = w_lhsT_load(w_dr, w_sb)
            vW = k.dma("gpsimd", bfr[0:1, :], BF_d[None, :], "wd")

            x3g_free = 0
            y_free = 0
            psF_free = [0, 0]
            psF_freeS = [0, 0]
            yT_T4 = [0]

            for g in range(NT // 4):
                k.wait("sync", "V", x3g_free)
                k.wait("sync", "st", y_free)
                vx = k.dma("sync", x3g[:, :, :], grp_view(Xs[3], g), "xd")
                ln_stats(x3g, vx, negmu4, rr4, 0, st8, sqscr, veps4)
                vy = ln_apply(x3g, y_bf, negmu4, rr4, 0, vx, yT_T4[0])
                vyT, yT_T4[0] = transpose_grp(y_bf, yTg, psT, vy)
                k.wait("tensor", "wd", vW)
                for (lhs_w, rhs_src, dstb) in ((Wf_bf, yTg, h1T),
                                               (A1_bf, h1T, h2T)):
                    k.wait("tensor", "V", vyT)
                    for dc in range(KC):
                        sl = dc % 2
                        k.wait("tensor", "V", psF_free[sl])
                        k.wait("tensor", "S", psF_freeS[sl])
                        for kc in range(KC):
                            vt = k.op("tensor", lambda E, kc=kc, dc=dc, sl=sl,
                                      lhs_w=lhs_w, rhs_src=rhs_src: E.matmul(
                                psF[:, sl, :],
                                lhs_w[:, kc, dc * 128:(dc + 1) * 128],
                                rhs_src[:, kc, :],
                                start=(kc == 0), stop=(kc == KC - 1)),
                                s="T" if kc == KC - 1 else None)
                        k.wait("vector", "T", vt)
                        psF_free[sl] = k.op(
                            "vector", lambda E, dc=dc, sl=sl, dstb=dstb:
                            E.tensor_copy(dstb[:, dc, :], psF[:, sl, :]), s="V")
                vres = 0
                for t in range(4):
                    for c in range(2):
                        sl = (t * 2 + c) % 2
                        k.wait("tensor", "V", psF_free[sl])
                        k.wait("tensor", "S", psF_freeS[sl])
                        for cc in range(KC):
                            k.op("tensor", lambda E, cc=cc, sl=sl, t=t, c=c:
                                 E.matmul(
                                psF[:, sl, :], h2T[:, cc, t * 128:(t + 1) * 128],
                                A2_bf[:, cc, c * 512:(c + 1) * 512],
                                start=(cc == 0), stop=False))
                        vt = k.op("tensor", lambda E, sl=sl, c=c: E.matmul(
                            psF[:, sl, :], ones_bf[0:1, 0:128],
                            bfr[0:1, c * 512:(c + 1) * 512],
                            start=False, stop=True), s="T")
                        k.wait("scalar", "T", vt)
                        k.wait("scalar", "V", vres)
                        vtan = k.op("scalar", lambda E, sl=sl: E.activation(
                            scrT[:, :], psF[:, sl, :], AF.Tanh), s="S")
                        psF_freeS[sl] = vtan
                        k.wait("vector", "S", vtan)
                        k.wait("vector", "xd", vx)
                        vres = k.op("vector", lambda E, t=t, c=c: E.tensor_tensor(
                            x3g[:, t, c * 512:(c + 1) * 512], scrT[:, :],
                            x3g[:, t, c * 512:(c + 1) * 512], ALU.add), s="V")
                        psF_free[sl] = vres
                x3g_free = vres
                k.wait("sync", "V", vres)
                y_free = k.dma("sync", grp_view(xdst(4), g), x3g[:, :, :], "st")
            k.flush(blk)

        k.wait("sync", "st", k.val("st"))
        k.flush(blk)
        return nc
    return nc


def make_inputs(inputs, core):
    x = np.asarray(inputs["L0_attn_x"], dtype=np.float32)
    xs = x[core * BPC:(core + 1) * BPC].reshape(TCORE, D)
    g = lambda n: np.ascontiguousarray(np.asarray(inputs[n], dtype=np.float32))
    return {
        "x0": np.ascontiguousarray(xs),
        "Wq": g("L0_attn_Wq"), "bq": g("L0_attn_bq"),
        "Wk": g("L0_attn_Wk"), "bk": g("L0_attn_bk"),
        "Wv": g("L0_attn_Wv"), "bv": g("L0_attn_bv"),
        "Wp": g("L0_attn_Wp"), "bp": g("L0_attn_bp"),
        "gw": g("L0_ffn_gate_w"), "w1": g("L0_ffn_w1"), "b1": g("L0_ffn_b1"),
        "lg1": g("L0_ffn_ln1_g"), "lb1": g("L0_ffn_ln1_b"),
        "w2": g("L0_ffn_w2"), "b2": g("L0_ffn_b2"),
        "lng": g("L0_ffn_ln_g"), "lnb": g("L0_ffn_ln_b"),
        "WK": g("L1_attn_W_K").reshape(D, D),
        "WQ": g("L1_attn_W_Q").reshape(D, D),
        "WV": g("L1_attn_W_V").reshape(D, D),
        "WO": g("L1_attn_W_O"),
        "Wf": g("L1_ffn_weight"), "A1": g("L1_ffn_aux_weight1"),
        "A2": g("L1_ffn_aux_weight2"), "bf": g("L1_ffn_bias"),
    }


def kernel(**inputs):
    nc = build()
    in_maps = [make_inputs(inputs, c) for c in range(NCORES)]
    res = run_bass_kernel_spmd(nc, in_maps, core_ids=list(range(NCORES)))
    out = np.concatenate([r["y"].reshape(BPC, SEQ, D) for r in res.results], axis=0)
    return out
